# version 31
# speedup vs baseline: 1.2637x; 1.0067x over previous
import contextlib
import math

import numpy as np
import ml_dtypes

import concourse.bass as bass
import concourse.mybir as mybir
from concourse.bass_utils import run_bass_kernel_spmd

F32 = mybir.dt.float32
BF16 = mybir.dt.bfloat16
I32 = mybir.dt.int32
U8 = mybir.dt.uint8
AF = mybir.ActivationFunctionType
ALU = mybir.AluOpType
AX = mybir.AxisListType

BF = ml_dtypes.bfloat16

EPS = 1e-5
C = 128
SEC = 512
WARM = 512
LNDK = -0.5 * math.log(128.0)

PJ0, PJ1 = 0, 1
PS2, PS3 = 2, 3
PSC = 4
PO5 = 5


class Em:

    ENGINES = ("te", "dve", "act", "pool", "sp")

    def __init__(self, nc):
        self.nc = nc
        self.ops = []
        self.cnt = {e: 0 for e in self.ENGINES}
        self.sems = {}
        self.dma_sems = {}
        self.dma_cnt = {}
        self.waited = {e: {} for e in self.ENGINES}
        self.last_dve_writes = frozenset()
        self.flush_fn = None

    def init_sems(self, stack, dma_names):
        for e in self.ENGINES:
            self.sems[e] = stack.enter_context(self.nc.semaphore(name="se_" + e))
        for n in dma_names:
            self.dma_sems[n] = stack.enter_context(self.nc.semaphore(name="sd_" + n))
            self.dma_cnt[n] = 0

    def _pre(self, engine, reads, writes):
        for t in reads:
            self._wait_tok(engine, t.w)
        for t in writes:
            self._wait_tok(engine, t.w)
            for tok in t.r.values():
                self._wait_tok(engine, tok)

    def _post(self, tok, reads, writes):
        for t in reads:
            t.r[tok[1] if tok[0] == "e" else "d:" + tok[1]] = tok
        for t in writes:
            t.w = tok
            t.r = {}

    def task(self, engine, fns, reads=(), writes=()):
        if engine == "dve" and self.flush_fn is not None:
            if any(id(t) in self.last_dve_writes for t in reads) or \
               any(id(t) in self.last_dve_writes for t in writes):
                fns = [self.flush_fn] + list(fns)
        self._pre(engine, reads, writes)
        self.cnt[engine] += 1
        tok = ("e", engine, self.cnt[engine])

        def wrapped(eng, fns=fns, engine=engine):
            inst = None
            for fn in fns:
                inst = fn(eng)
            inst.then_inc(self.sems[engine], 1)
        self.ops.append((engine, wrapped))
        self._post(tok, reads, writes)
        if engine == "dve":
            self.last_dve_writes = frozenset(id(t) for t in writes)
        return tok

    def dma(self, queue, sem, out, in_, reads=(), writes=(), transpose=False, **kw):
        self._pre(queue, reads, writes)
        self.dma_cnt[sem] += 16
        tok = ("d", sem, self.dma_cnt[sem])

        def wrapped(eng, out=out, in_=in_, sem=sem, kw=kw, transpose=transpose):
            if transpose:
                inst = eng.dma_start_transpose(out=out, in_=in_)
            else:
                inst = eng.dma_start(out=out, in_=in_, **kw)
            inst.then_inc(self.dma_sems[sem], 16)
        self.ops.append((queue, wrapped))
        self._post(tok, reads, writes)
        return tok

    def _wait_tok(self, engine, tok):
        if tok is None:
            return
        if tok[0] == "e":
            _, e, v = tok
            if e == engine:
                return
            key = "e:" + e
            if self.waited[engine].get(key, -1) >= v:
                return
            self.waited[engine][key] = v

            def wrapped(eng, e=e, v=v):
                eng.wait_ge(self.sems[e], v)
            self.ops.append((engine, wrapped))
        else:
            _, sem, v = tok
            key = "d:" + sem
            if self.waited[engine].get(key, -1) >= v:
                return
            self.waited[engine][key] = v

            def wrapped(eng, sem=sem, v=v):
                eng.wait_ge(self.dma_sems[sem], v)
            self.ops.append((engine, wrapped))

    def wait_all_dma(self, engine, sem):
        self._wait_tok(engine, ("d", sem, self.dma_cnt[sem]))

    def barrier(self):
        toks = [("e", e, self.cnt[e]) for e in self.ENGINES if self.cnt[e] > 0]
        toks += [("d", s, c) for s, c in self.dma_cnt.items() if c > 0]
        for e in self.ENGINES:
            for t in toks:
                self._wait_tok(e, t)

    def replay(self, block):
        handles = {"te": block.tensor, "dve": block.vector, "act": block.scalar,
                   "pool": block.gpsimd, "sp": block.sync}
        for e in self.ENGINES:
            fns = [fn for ee, fn in self.ops if ee == e]
            if not fns:
                continue

            def body(eng, fns=fns):
                for fn in fns:
                    fn(eng)
            handles[e](body)


class Trk:
    def __init__(self):
        self.w = None
        self.r = {}


def _pairs(lst):
    return [tuple(lst[i:i + 2]) for i in range(0, len(lst), 2)]


def build_nc(H=2048, HF=8192, TK=2048, dump=False, dbg=0):
    NH = H // 128
    DC = H // 128
    TF = TK + WARM
    NSEC = TF // SEC
    NTC = TK // 128
    NTF = TF // 128
    HC = HF // 128
    HB = 1024
    NHB = HF // HB
    HCB = HB // 128
    OC = max(1, H // 512)
    NTQ = TK // 512
    SCB = 64
    PCU = min(2048, H)
    PCD = min(2048, HF)

    nc = bass.Bass("TRN2", target_bir_lowering=False, debug=False,
                   detect_race_conditions=False)

    dri = {}
    for nm, shp, dt in [
        ("x_att", [TF, H], F32), ("wscale", [128, 1], F32),
        ("identb", [128, 128], BF16), ("masku8", [128, SEC], U8),
        ("Wq", [H, H], F32), ("Wf", [H, H], F32), ("Wi", [H, H], F32),
        ("Wo", [H, H], F32), ("gw_t", [128, DC], F32),
        ("b_up_t", [128, HC], F32), ("b_dn_bc", [128, H], F32),
        ("A_up_t", [128, DC * 16], F32), ("B_up", [16, HF], F32),
        ("A_dn_t", [128, HC * 16], F32), ("B_dn", [16, H], F32),
        ("wuq", [HF, H], I32), ("wus", [HF, H // SCB], F32),
        ("wdq", [H, HF], I32), ("wds", [H, HF // SCB], F32),
    ]:
        dri[nm] = nc.dram_tensor(nm, shp, dt, kind="ExternalInput").ap()
    out_d = nc.dram_tensor("out", [TK, H], F32, kind="ExternalOutput").ap()

    ik = "ExternalOutput" if dump else "Internal"
    wup_nat = nc.dram_tensor("wup_nat", [HF, H], BF16, kind=ik).ap()
    wdn_nat = nc.dram_tensor("wdn_nat", [H, HF], BF16, kind=ik).ap()
    res_sc = nc.dram_tensor("res_sc", [NH, 128, TK], BF16, kind=ik).ap()
    y_res = nc.dram_tensor("y_res", [TK, H], F32, kind=ik).ap()
    y2_hbm = nc.dram_tensor("y2_hbm", [TK, H], F32, kind=ik).ap()
    if dbg:
        dxn = nc.dram_tensor("dxn", [DC * 128, TF], F32, kind="ExternalOutput").ap()
        dslot = nc.dram_tensor("dslot", [16, 128, SEC], F32, kind="ExternalOutput").ap()
        dst1 = nc.dram_tensor("dst1", [NTF, 128, 8], F32, kind="ExternalOutput").ap()
        dyn = nc.dram_tensor("dyn", [DC * 128, TK], F32, kind="ExternalOutput").ap()
        dx2 = nc.dram_tensor("dx2", [HB, TK], F32, kind="ExternalOutput").ap()
        dt1 = nc.dram_tensor("dt1", [16, TK], F32, kind="ExternalOutput").ap()
        dwu = nc.dram_tensor("dwu", [DC * 128, HB], F32, kind="ExternalOutput").ap()

    em = Em(nc)
    dma_names = ["cst", "xin0", "xin1", "ws0", "ws1", "resw0", "resw1",
                 "dq0", "dq1", "natw", "wo0", "wo1", "resr0", "resr1",
                 "xre0", "xre1", "yw", "yr0", "yr1", "wup", "wdn", "y2a",
                 "y2r0", "y2r1", "yrf0", "yrf1", "outw", "lor"]

    with contextlib.ExitStack() as top:
        em.init_sems(top, dma_names)

        def sbuf(stack, name, shp, dt=F32):
            return stack.enter_context(nc.sbuf_tensor(name, shp, dt)).ap()

        identb = sbuf(top, "c_identb", [128, 128], BF16)
        masks = sbuf(top, "c_mask", [128, SEC], U8)
        wsc = sbuf(top, "c_wsc", [128, 1], F32)
        ones_col = sbuf(top, "c_onesc", [128, 1], BF16)
        ones_row = sbuf(top, "c_onesr", [1, 128], BF16)
        gw_sb = sbuf(top, "c_gw", [128, DC], F32)
        cst_eps = sbuf(top, "c_eps", [128, 1], F32)
        cst_one = sbuf(top, "c_one", [128, 1], F32)
        cst_lndk = sbuf(top, "c_lndk", [128, 1], F32)
        m2r2 = sbuf(top, "c_m2r2", [128, 2 * NTC], F32)
        flsh = sbuf(top, "c_flsh", [128, 8], F32)
        t_identb, t_masks, t_wsc = Trk(), Trk(), Trk()
        t_onec, t_oner, t_gw, t_m2r2 = Trk(), Trk(), Trk(), Trk()

        pb = [top.enter_context(nc.psum_tensor(f"pb{i}", [128, 512], F32)).ap()
              for i in range(6)]
        pba = top.enter_context(nc.psum_tensor("pba", [128, 1024], BF16)).ap()
        pbb = top.enter_context(nc.psum_tensor("pbb", [128, 1024], BF16)).ap()
        t_pb = [Trk() for _ in range(6)]
        t_pba, t_pbb = Trk(), Trk()

        em.task("dve", [lambda v: v.memset(flsh, 0.0)])
        em.flush_fn = lambda v: v.tensor_copy(flsh[:, 4:8], flsh[:, 0:4])
        em.dma("sp", "cst", identb, dri["identb"], writes=[t_identb])
        em.dma("sp", "cst", masks, dri["masku8"], writes=[t_masks])
        em.dma("sp", "cst", wsc, dri["wscale"], writes=[t_wsc])
        em.dma("sp", "cst", gw_sb, dri["gw_t"], writes=[t_gw])
        t_eps, t_lndk = Trk(), Trk()
        em.task("dve", [lambda v: v.memset(cst_eps, float(EPS))], writes=[t_eps])
        t_one = Trk()
        em.task("dve", [lambda v: v.memset(cst_one, 1.0)], writes=[t_one])
        em.task("dve", [lambda v: v.memset(cst_lndk, float(LNDK))], writes=[t_lndk])
        em.task("dve", [lambda v: v.memset(ones_col, 1.0)], writes=[t_onec])
        em.task("dve", [lambda v: v.memset(ones_row, 1.0)], writes=[t_oner])

        with contextlib.ExitStack() as ph12:
            ssq = sbuf(ph12, "s_ssq", [1, TK], F32)
            ssqb = sbuf(ph12, "s_ssqb", [1, TK], BF16)
            t_ssq, t_ssqb = Trk(), Trk()
            dq_q = [sbuf(ph12, f"s_dqq{i}", [128, 1024], I32) for i in range(2)]
            dq_s = [sbuf(ph12, f"s_dqsc{i}", [128, 16], F32) for i in range(2)]
            dq_o = [sbuf(ph12, f"s_dqo{i}", [128, 1024], BF16) for i in range(2)]
            t_dqq = [Trk(), Trk()]
            t_dqs = [Trk(), Trk()]
            t_dqo = [Trk(), Trk()]

            with contextlib.ExitStack() as ph1:
                hstate = sbuf(ph1, "s_h", [128, 2 * 128], F32)
                t_h = [Trk() for _ in range(2)]
                zeros128 = sbuf(ph1, "s_zeros", [128, 128], F32)
                zeros512 = sbuf(ph1, "s_z512", [128, SEC], BF16)
                t_zero, t_z512 = Trk(), Trk()
                em.task("dve", [lambda v: v.memset(zeros128, 0.0)], writes=[t_zero])
                em.task("dve", [lambda v: v.memset(zeros512, 0.0)], writes=[t_z512])
                em.task("dve", [lambda v: v.memset(hstate, 0.0)], writes=list(t_h))
                em.task("dve", [lambda v: v.memset(ssq, 0.0)], writes=[t_ssq])

                xn_fm = [sbuf(ph1, f"s_xnfm{dc}", [128, TF], BF16) for dc in range(DC)]
                t_xnfm = [Trk() for _ in range(DC)]
                ln1s = contextlib.ExitStack()
                x_ld = [sbuf(ln1s, f"s_xld{i}", [128, H], F32) for i in range(2)]
                t_xld = [Trk(), Trk()]
                xn_tm = sbuf(ln1s, "s_xntm", [128, H], BF16)
                t_xntm = Trk()
                sqj = sbuf(ln1s, "s_sqj", [128, H], BF16)
                cols = sbuf(ln1s, "s_cols", [128, 8], F32)
                t_cols = Trk()

                def ln_cols(xa, t_xa, n):
                    em.task("act",
                            [lambda s: s.activation(sqj[:, 0:n], xa, AF.Square,
                                                    accum_out=cols[:, 0:1])],
                            reads=[t_xa], writes=[t_cols])
                    em.task("dve",
                            [lambda v: v.reduce_sum(cols[:, 1:2], xa, axis=AX.X),
                             em.flush_fn,
                             lambda v: v.tensor_scalar_mul(cols[:, 2:3], cols[:, 1:2], 1.0 / n),
                             lambda v: v.tensor_scalar_mul(cols[:, 3:4], cols[:, 0:1], 1.0 / n),
                             em.flush_fn,
                             lambda v: v.tensor_mul(cols[:, 4:5], cols[:, 2:3], cols[:, 2:3]),
                             em.flush_fn,
                             lambda v: v.tensor_sub(cols[:, 3:4], cols[:, 3:4], cols[:, 4:5])],
                            reads=[t_xa, t_cols], writes=[t_cols])
                    em.task("act",
                            [lambda s: s.activation(cols[:, 5:6], cols[:, 3:4], AF.Ln,
                                                    bias=cst_eps),
                             lambda s: s.copy(cols[:, 7:8], cols[:, 0:1]),
                             lambda s: s.activation(cols[:, 6:7], cols[:, 5:6], AF.Exp,
                                                    scale=-0.5)],
                            reads=[t_cols, t_eps], writes=[t_cols])

                for tc in range(NTF):
                    sl = tc % 2
                    em.dma("sp", f"xin{sl}", x_ld[sl],
                           dri["x_att"][tc * 128:(tc + 1) * 128, :], writes=[t_xld[sl]])
                    ln_cols(x_ld[sl], t_xld[sl], H)
                    em.task("dve",
                            [lambda v, sl=sl: v.tensor_scalar(
                                xn_tm, x_ld[sl], cols[:, 2:3], cols[:, 6:7],
                                ALU.subtract, ALU.mult)],
                            reads=[t_xld[sl], t_cols], writes=[t_xntm])
                    if dbg:
                        em.dma("pool", "natw", dst1[tc, :, :], cols, reads=[t_cols])
                    for g0 in range(0, DC, 8):
                        gdc = list(range(g0, min(g0 + 8, DC)))
                        pt = (pba, t_pba) if (g0 // 8) % 2 == 0 else (pbb, t_pbb)
                        em.task("te",
                                [(lambda t, dc=dc, j=j, pt=pt:
                                  t.transpose(pt[0][:, j * 128:(j + 1) * 128],
                                              xn_tm[:, dc * 128:(dc + 1) * 128], identb))
                                 for j, dc in enumerate(gdc)],
                                reads=[t_xntm, t_identb], writes=[pt[1]])
                        em.task("act",
                                [(lambda s, dc=dc, j=j, tc=tc, pt=pt:
                                  s.copy(xn_fm[dc][:, tc * 128:(tc + 1) * 128],
                                         pt[0][:, j * 128:(j + 1) * 128]))
                                 for j, dc in enumerate(gdc)],
                                reads=[pt[1]], writes=[t_xnfm[dc] for dc in gdc])

                if dbg:
                    for dc in range(DC):
                        em.dma("pool", "natw", dxn[dc * 128:(dc + 1) * 128, :],
                               xn_fm[dc], reads=[t_xnfm[dc]])

                ln1s.close()
                em.barrier()

                wstr = [sbuf(ph1, f"s_wstr{i}", [128, 3 * H], BF16) for i in range(2)]
                t_wstr = [Trk(), Trk()]

                class Slot:
                    pass

                slots = []
                for i in range(3):
                    w = Slot()
                    for nm, dt in (("s", F32), ("g", F32), ("cum", F32),
                                   ("qs", BF16), ("e1", BF16), ("e2", BF16),
                                   ("k", BF16), ("qt", BF16), ("kt", BF16),
                                   ("kh", BF16), ("v", BF16), ("vt", BF16),
                                   ("kht", BF16)):
                        setattr(w, nm, sbuf(ph1, f"ws_{nm}{i}", [128, SEC], dt))
                        setattr(w, "t_" + nm, Trk())
                    for nm in ("ccols", "expc", "scc", "dcol"):
                        setattr(w, nm, sbuf(ph1, f"ws_{nm}{i}", [128, SEC // C], F32))
                        setattr(w, "t_" + nm, Trk())
                    w.hh = sbuf(ph1, f"ws_hh{i}", [128, SEC], F32)
                    w.t_hh = Trk()
                    w.whp = sbuf(ph1, f"ws_whp{i}", [128, SEC], BF16)
                    w.t_whp = Trk()
                    w.stt = sbuf(ph1, f"ws_stt{i}", [128, SEC], BF16)
                    w.t_stt = Trk()
                    slots.append(w)
                sq_sh = sbuf(ph1, "s_sqsh", [128, SEC], BF16)
                t_sqsh = Trk()
                o_head = [sbuf(ph1, f"s_oh{i}", [128, TK], BF16) for i in range(2)]
                t_oh = [Trk(), Trk()]

                def w_stream(h):
                    sl = h % 2
                    for j, wt in enumerate((dri["Wq"], dri["Wf"], dri["Wi"])):
                        src = wt.rearrange("(a p) c -> p a c", p=128)[:, :, h * 128:(h + 1) * 128]
                        em.dma("pool", f"ws{sl}",
                               wstr[sl][:, j * H:(j + 1) * H]
                               .rearrange("p (a c) -> p a c", c=128),
                               src, writes=[t_wstr[sl]] if j == 0 else [],
                               reads=[] if j == 0 else [])
                    t_wstr[sl].w = ("d", f"ws{sl}", em.dma_cnt[f"ws{sl}"])

                def proj_group(hp, wj, secs, banks):
                    for j, s in enumerate(secs):
                        fns = []
                        for dc in range(DC):
                            wcol = wj * H + dc * 128
                            fns.append(lambda t, wcol=wcol, hp=hp:
                                       t.ldweights(wstr[hp][:, wcol:wcol + 128]))
                            fns.append(lambda t, wcol=wcol, hp=hp, j=j, s=s, dc=dc:
                                       t.matmul(pb[banks[j]],
                                                wstr[hp][:, wcol:wcol + 128],
                                                xn_fm[dc][:, s * SEC:(s + 1) * SEC],
                                                start=(dc == 0), stop=(dc == DC - 1)))
                        em.task("te", fns,
                                reads=[t_wstr[hp]] + [t_xnfm[dc] for dc in range(DC)],
                                writes=[t_pb[banks[j]]])

                def sec_elementwise(s, h):
                    w = slots[(h * NSEC + s) % 3]
                    warm = (s == 0)
                    fns = []
                    if not warm:
                        fns.append(lambda v, w=w: v.tensor_mul(w.qs, w.qs, w.e1))
                    fns.append(lambda v, w=w: v.tensor_mul(w.k, w.s, w.kht))
                    fns += [lambda v, w=w, c=c: v.tensor_tensor_scan(
                        w.cum[:, c * C:(c + 1) * C], w.g[:, c * C:(c + 1) * C],
                        zeros128, 0.0, ALU.add, ALU.add) for c in range(SEC // C)]
                    fns += [em.flush_fn,
                            lambda v, w=w: v.tensor_copy(w.ccols, w.cum[:, C // 2 - 1::C]),
                            em.flush_fn,
                            lambda v, w=w: v.tensor_tensor(
                                w.cum.rearrange("p (a c) -> p a c", c=C),
                                w.cum.rearrange("p (a c) -> p a c", c=C),
                                w.ccols.unsqueeze(2).broadcast_to([128, SEC // C, C]),
                                ALU.subtract),
                            em.flush_fn,
                            lambda v, w=w: v.tensor_copy(w.scc, w.cum[:, C - 1::C])]
                    em.task("dve", fns, reads=[w.t_s, w.t_g, t_zero, w.t_e1,
                                               w.t_qs, w.t_kht],
                            writes=[w.t_cum, w.t_ccols, w.t_scc, w.t_k, w.t_e1,
                                    w.t_qs])
                    fns = [lambda sc, w=w: sc.activation(w.e2, w.cum, AF.Exp)]
                    if not warm:
                        fns.append(lambda sc, w=w: sc.activation(w.e1, w.cum, AF.Exp,
                                                                 scale=-1.0,
                                                                 bias=cst_lndk))
                    fns.append(lambda sc, w=w: sc.activation(w.expc, w.ccols, AF.Exp,
                                                             scale=-1.0))
                    fns.append(lambda sc, w=w: sc.activation(w.scc, w.scc, AF.Exp,
                                                             scale=-1.0))
                    em.task("act", fns, reads=[w.t_cum, w.t_ccols, w.t_scc, t_lndk],
                            writes=[w.t_e1, w.t_e2, w.t_expc, w.t_scc])
                    fns = []
                    if not warm:
                        fns.append(lambda v, w=w: v.tensor_mul(w.qt, w.qs, w.e1))
                    fns.append(lambda v, w=w: v.tensor_mul(w.kt, w.k, w.e2))
                    if warm:
                        fns.append(lambda v, w=w: v.tensor_scalar(
                            w.scc, w.scc, wsc[:, 0:1], None, ALU.mult))
                    fns.append(lambda v, w=w: v.tensor_mul(w.dcol, w.scc, w.expc))
                    fns.append(em.flush_fn)
                    fns.append(lambda v, w=w: v.tensor_tensor(
                        w.kh.rearrange("p (a c) -> p a c", c=C),
                        w.kt.rearrange("p (a c) -> p a c", c=C),
                        w.scc.unsqueeze(2).broadcast_to([128, SEC // C, C]),
                        ALU.mult))
                    em.task("dve", fns,
                            reads=[w.t_qs, w.t_e1, w.t_e2, w.t_k, w.t_expc,
                                   w.t_scc, t_wsc],
                            writes=[w.t_qt, w.t_kt, w.t_kh, w.t_dcol, w.t_scc])

                def sec_chunks(s, h):
                    hp = h % 2
                    w = slots[(h * NSEC + s) % 3]
                    warm = (s == 0)
                    NCH = SEC // C
                    em.task("te",
                            [(lambda t, w=w, c=c:
                              t.transpose(pba[:, c * 128:(c + 1) * 128],
                                          w.v[:, c * C:(c + 1) * C], identb))
                             for c in range(NCH)],
                            reads=[w.t_v, t_identb], writes=[t_pba])
                    em.task("act",
                            [lambda sc, w=w: sc.copy(w.vt, pba[:, 0:SEC])],
                            reads=[t_pba], writes=[w.t_vt])
                    if not warm:
                        fns = []
                        for c in range(NCH):
                            fns.append(lambda t, w=w, c=c:
                                       t.ldweights(w.kt[:, c * C:(c + 1) * C]))
                            fns.append(lambda t, w=w, c=c:
                                       t.matmul(pb[PSC][:, c * 128:(c + 1) * 128],
                                                w.kt[:, c * C:(c + 1) * C],
                                                w.qt[:, c * C:(c + 1) * C],
                                                start=True, stop=True))
                        em.task("te", fns, reads=[w.t_kt, w.t_qt], writes=[t_pb[PSC]])
                    em.task("te",
                            [(lambda t, w=w, c=c:
                              t.transpose(pbb[:, c * 128:(c + 1) * 128],
                                          w.kh[:, c * C:(c + 1) * C], identb))
                             for c in range(NCH)],
                            reads=[w.t_kh, t_identb], writes=[t_pbb])
                    em.task("act",
                            [lambda sc, w=w: sc.copy(w.kht, pbb[:, 0:SEC])],
                            reads=[t_pbb], writes=[w.t_kht])
                    if not warm:
                        em.task("dve",
                                [lambda v, w=w: v.select(
                                    w.stt, masks, pb[PSC][:, 0:SEC],
                                    zeros512)],
                                reads=[t_pb[PSC], t_masks, t_z512], writes=[w.t_stt])
                    fns = []
                    for c in range(NCH):
                        fns.append(lambda t, w=w, c=c:
                                   t.ldweights(w.kht[:, c * C:(c + 1) * C]))
                        fns.append(lambda t, w=w, c=c:
                                   t.matmul(pb[PSC][:, c * 128:(c + 1) * 128],
                                            w.kht[:, c * C:(c + 1) * C],
                                            w.vt[:, c * C:(c + 1) * C],
                                            start=True, stop=True))
                    em.task("te", fns, reads=[w.t_kht, w.t_vt], writes=[t_pb[PSC]])
                    fns = []
                    hsl = hstate[:, (h % 2) * 128:(h % 2 + 1) * 128]
                    for c in range(NCH):
                        if c == 0:
                            srcap = zeros128 if s == 0 else hsl
                        else:
                            srcap = w.hh[:, (c - 1) * 128:c * 128]
                        if not warm:
                            fns.append(lambda v, w=w, c=c, srcap=srcap:
                                       v.tensor_scalar(w.whp[:, c * 128:(c + 1) * 128],
                                                       srcap, w.expc[:, c:c + 1], None,
                                                       ALU.mult))
                        dstap = hsl if c == NCH - 1 \
                            else w.hh[:, c * 128:(c + 1) * 128]
                        fns.append(lambda v, w=w, c=c, srcap=srcap, dstap=dstap:
                                   v.scalar_tensor_tensor(
                                       dstap, srcap, w.dcol[:, c:c + 1],
                                       pb[PSC][:, c * 128:(c + 1) * 128],
                                       ALU.mult, ALU.add))
                        fns.append(em.flush_fn)
                    em.task("dve", fns[:-1],
                            reads=[t_pb[PSC], w.t_dcol, w.t_expc, t_h[h % 2], t_zero],
                            writes=[w.t_hh, w.t_whp, t_h[h % 2]])
                    if warm:
                        return
                    fns = []
                    for c in range(NCH):
                        fns.append(lambda t, w=w, c=c:
                                   t.ldweights(w.vt[:, c * C:(c + 1) * C]))
                        fns.append(lambda t, w=w, c=c:
                                   t.matmul(pb[PO5][:, c * 128:(c + 1) * 128],
                                            w.vt[:, c * C:(c + 1) * C],
                                            w.stt[:, c * 128:(c + 1) * 128],
                                            start=True, stop=False))
                        fns.append(lambda t, w=w, c=c:
                                   t.ldweights(w.whp[:, c * 128:(c + 1) * 128]))
                        fns.append(lambda t, w=w, c=c:
                                   t.matmul(pb[PO5][:, c * 128:(c + 1) * 128],
                                            w.whp[:, c * 128:(c + 1) * 128],
                                            w.qt[:, c * C:(c + 1) * C],
                                            start=False, stop=True))
                    em.task("te", fns, reads=[w.t_vt, w.t_stt, w.t_whp, w.t_qt],
                            writes=[t_pb[PO5]])
                    kk = s * SEC - WARM
                    em.task("act",
                            [lambda sc, hp=hp, kk=kk:
                             sc.copy(o_head[hp][:, kk:kk + SEC], pb[PO5][:, 0:SEC]),
                             lambda sc, hp=hp, kk=kk:
                             sc.activation(sq_sh, o_head[hp][:, kk:kk + SEC], AF.Square)],
                            reads=[t_pb[PO5], t_oh[hp]], writes=[t_oh[hp], t_sqsh])
                    em.task("te",
                            [lambda t: t.ldweights(ones_col),
                             lambda t: t.matmul(pb[PO5][0:1, 0:SEC], ones_col,
                                                sq_sh, start=True, stop=True)],
                            reads=[t_sqsh, t_onec], writes=[t_pb[PO5]])
                    em.task("dve",
                            [lambda v, kk=kk: v.tensor_add(
                                ssq[0:1, kk:kk + SEC], ssq[0:1, kk:kk + SEC],
                                pb[PO5][0:1, 0:SEC])],
                            reads=[t_pb[PO5]], writes=[t_ssq])

                deq_jobs = []
                for r in range(HF // 128):
                    for p in range(H // PCU):
                        deq_jobs.append(("up", r, p))
                for r in range(H // 128):
                    for p in range(HF // PCD):
                        deq_jobs.append(("dn", r, p))
                deq_pieces = []
                for j in range(len(deq_jobs)):
                    kind, r, p = deq_jobs[j]
                    pc = PCU if kind == "up" else PCD
                    for off in range(0, pc, 1024):
                        deq_pieces.append((j, off))
                deq_cursor = [0]

                def emit_deq_n(n):
                    for _ in range(n):
                        if deq_cursor[0] >= len(deq_pieces):
                            return
                        j, off = deq_pieces[deq_cursor[0]]
                        emit_piece(j, off, deq_cursor[0] % 2)
                        deq_cursor[0] += 1

                def emit_piece(j, off, sl):
                    kind, r, p = deq_jobs[j]
                    if kind == "up":
                        pc = PCU
                        qsrc = dri["wuq"][r * 128:(r + 1) * 128, p * pc:(p + 1) * pc]
                        ssrc = dri["wus"][r * 128:(r + 1) * 128,
                                          p * (pc // SCB):(p + 1) * (pc // SCB)]
                        ndst = wup_nat[r * 128:(r + 1) * 128, p * pc:(p + 1) * pc]
                    else:
                        pc = PCD
                        qsrc = dri["wdq"][r * 128:(r + 1) * 128, p * pc:(p + 1) * pc]
                        ssrc = dri["wds"][r * 128:(r + 1) * 128,
                                          p * (pc // SCB):(p + 1) * (pc // SCB)]
                        ndst = wdn_nat[r * 128:(r + 1) * 128, p * pc:(p + 1) * pc]
                    n = min(1024, pc - off)
                    nbl = n // SCB
                    em.dma("sp", f"dq{sl}", dq_q[sl][:, 0:n],
                           qsrc[:, off:off + n], writes=[t_dqq[sl]])
                    em.dma("sp", f"dq{sl}", dq_s[sl][:, 0:nbl],
                           ssrc[:, off // SCB:off // SCB + nbl], writes=[t_dqs[sl]])
                    em.task("dve",
                            [lambda v, sl=sl, n=n, nbl=nbl: v.scalar_tensor_tensor(
                                dq_o[sl][:, 0:n].rearrange("p (a c) -> p a c", c=SCB),
                                dq_q[sl][:, 0:n].rearrange("p (a c) -> p a c", c=SCB),
                                -7.5,
                                dq_s[sl][:, 0:nbl].unsqueeze(2)
                                .broadcast_to([128, nbl, SCB]),
                                ALU.add, ALU.mult)],
                            reads=[t_dqq[sl], t_dqs[sl]], writes=[t_dqo[sl]])
                    em.dma("act", "natw", ndst[:, off:off + n], dq_o[sl][:, 0:n],
                           reads=[t_dqo[sl]])

                def emit_proj_wave(h, pr):
                    hp = h % 2
                    banks = [PJ0, PJ1][:len(pr)]
                    ibanks = [PS2, PS3][:len(pr)]
                    proj_group(hp, 1, pr, banks)
                    for j, s in enumerate(pr):
                        w = slots[(h * NSEC + s) % 3]
                        em.task("act",
                                [lambda sc, w=w, bk=banks[j]:
                                 sc.activation(w.s, pb[bk], AF.Exp, scale=-1.0),
                                 lambda sc, w=w:
                                 sc.activation(w.g, w.s, AF.Ln, bias=cst_one),
                                 lambda sc, w=w:
                                 sc.activation(w.kht, w.g, AF.Exp, scale=-1.0)],
                                reads=[t_pb[banks[j]], t_one],
                                writes=[w.t_s, w.t_g, w.t_kht])
                    proj_group(hp, 2, pr, ibanks)
                    for j, s in enumerate(pr):
                        w = slots[(h * NSEC + s) % 3]
                        em.task("act",
                                [lambda sc, w=w, bk=ibanks[j]:
                                 sc.copy(w.v, pb[bk])],
                                reads=[t_pb[ibanks[j]]], writes=[w.t_v])
                    qsecs = [s for s in pr if s >= 1]
                    if qsecs:
                        qbanks = [PJ0, PJ1][:len(qsecs)]
                        proj_group(hp, 0, qsecs, qbanks)
                        for j, s in enumerate(qsecs):
                            w = slots[(h * NSEC + s) % 3]
                            em.task("act",
                                    [lambda sc, w=w, bk=qbanks[j]:
                                     sc.copy(w.qs, pb[bk]),
                                     lambda sc, w=w, bk=qbanks[j]:
                                     sc.activation(w.e1, pb[bk], AF.Exp, scale=-1.0),
                                     lambda sc, w=w:
                                     sc.activation(w.cum, w.e1, AF.Ln, bias=cst_one),
                                     lambda sc, w=w:
                                     sc.activation(w.e1, w.cum, AF.Exp, scale=-1.0)],
                                    reads=[t_pb[qbanks[j]], t_one],
                                    writes=[w.t_qs, w.t_e1, w.t_cum])

                def process_sec(h, s):
                    hp = h % 2
                    sec_elementwise(s, h)
                    sec_chunks(s, h)
                    emit_deq_n(3)
                    if s == NSEC - 1:
                        em.dma("act", f"resw{hp}", res_sc[h, :, :], o_head[hp],
                               reads=[t_oh[hp]])

                waves = [(h, pr) for h in range(NH)
                         for pr in _pairs(list(range(NSEC)))]
                w_stream(0)
                pending = []
                for (h, pr) in waves:
                    if pr[0] == 0 and h + 1 < NH:
                        w_stream(h + 1)
                    emit_proj_wave(h, pr)
                    pending += [(h, s) for s in pr]
                    while len(pending) > 1:
                        process_sec(*pending.pop(0))
                while pending:
                    process_sec(*pending.pop(0))

                em.deq_n = emit_deq_n
                em.task("dve",
                        [lambda v: v.tensor_scalar_mul(ssq, ssq, 1.0 / H)],
                        writes=[t_ssq])
                em.task("act",
                        [lambda sc: sc.activation(ssq, ssq, AF.Sqrt, bias=cst_eps[0:1, :])],
                        reads=[t_eps], writes=[t_ssq])
                em.task("dve",
                        [lambda v: v.reciprocal(ssq, ssq),
                         em.flush_fn,
                         lambda v: v.tensor_copy(ssqb, ssq)],
                        writes=[t_ssq, t_ssqb])

            em.barrier()

            with contextlib.ExitStack() as ph2:
                rstd_bc = sbuf(ph2, "s_rstdbc", [128, TK], BF16)
                t_rstdbc = Trk()
                resn = [sbuf(ph2, f"s_resn{h2}", [128, TK], BF16) for h2 in range(NH)]
                t_resn = [Trk() for _ in range(NH)]
                wob = [sbuf(ph2, f"s_wob{d}", [128, H], BF16) for d in range(DC)]
                t_wob = [Trk() for _ in range(DC)]

                ao = [sbuf(ph2, f"s_ao{i}", [128, H], F32) for i in range(2)]
                t_ao = [Trk(), Trk()]
                xt = [sbuf(ph2, f"s_xt{i}", [128, H], F32) for i in range(2)]
                t_xt = [Trk(), Trk()]
                ytmp = [sbuf(ph2, f"s_ytmp{i}", [128, H], F32) for i in range(2)]
                t_ytmp = [Trk(), Trk()]
                sqj2 = sbuf(ph2, "s_sqj2", [128, 512], BF16)
                cols2p = [sbuf(ph2, f"s_cols2{i}", [128, 16], F32) for i in range(2)]
                t_cols2p = [Trk(), Trk()]

                for j in range(TK // 512):
                    em.task("te",
                            [lambda t: t.ldweights(ones_row),
                             lambda t, j=j: t.matmul(pb[PJ0], ones_row,
                                                     ssqb[0:1, j * 512:(j + 1) * 512],
                                                     start=True, stop=True)],
                            reads=[t_ssqb, t_oner], writes=[t_pb[PJ0]])
                    em.task("act",
                            [lambda sc, j=j: sc.copy(rstd_bc[:, j * 512:(j + 1) * 512],
                                                     pb[PJ0])],
                            reads=[t_pb[PJ0]], writes=[t_rstdbc])

                for dvc in range(NH):
                    sl = dvc % 2
                    em.dma("sp", f"resr{sl}", resn[dvc], res_sc[dvc, :, :],
                           writes=[t_resn[dvc]])
                    em.dma("pool", f"wo{sl}", wob[dvc],
                           dri["Wo"][dvc * 128:(dvc + 1) * 128, :],
                           writes=[t_wob[dvc]])
                    em.task("dve",
                            [lambda v, dvc=dvc: v.tensor_mul(resn[dvc], resn[dvc],
                                                             rstd_bc)],
                            reads=[t_rstdbc], writes=[t_resn[dvc]])
                    em.task("dve",
                            [lambda v, dvc=dvc: v.tensor_scalar(
                                wob[dvc], wob[dvc], gw_sb[:, dvc:dvc + 1], None,
                                ALU.mult)],
                            reads=[t_gw], writes=[t_wob[dvc]])

                for tc in range(NTC):
                    sl = tc % 2
                    CC = cols2p[sl]
                    t_CC = t_cols2p[sl]
                    em.deq_n(3)
                    em.dma("sp", f"xre{sl}", xt[sl],
                           dri["x_att"][(WARM + tc * 128):(WARM + (tc + 1) * 128), :],
                           writes=[t_xt[sl]])
                    oc_pairs = ([[0, 1], [2, 3]] if OC >= 4
                                else [list(range(OC))])
                    for hgrp in range(2):
                        lo, hi = (0, NH // 2) if hgrp == 0 else (NH // 2, NH)
                        for ocs in oc_pairs:
                            fns = []
                            for dvc in range(lo, hi):
                                fns.append(lambda t, dvc=dvc, tc=tc:
                                           t.ldweights(
                                               resn[dvc][:, tc * 128:(tc + 1) * 128]))
                                for oc in ocs:
                                    fns.append(lambda t, dvc=dvc, oc=oc, tc=tc:
                                               t.matmul(pb[oc],
                                                        resn[dvc][:, tc * 128:
                                                                  (tc + 1) * 128],
                                                        wob[dvc][:, oc * 512:
                                                                 (oc + 1) * 512],
                                                        start=(dvc == 0),
                                                        stop=(dvc == NH - 1)))
                            em.task("te", fns,
                                    reads=[t_resn[d] for d in range(lo, hi)] +
                                    [t_wob[d] for d in range(lo, hi)],
                                    writes=[t_pb[oc] for oc in ocs])
                    for ocs in oc_pairs:
                        em.task("act",
                                [(lambda sc, sl=sl, oc=oc:
                                  sc.activation(ao[sl][:, oc * 512:(oc + 1) * 512],
                                                pb[oc], AF.Copy,
                                                accum_out=CC[:, oc:oc + 1]))
                                 for oc in ocs],
                                reads=[t_pb[oc] for oc in ocs],
                                writes=[t_ao[sl], t_CC])
                    em.task("act",
                            [(lambda sc, sl=sl, oc=oc:
                              sc.activation(sqj2[:, 0:512],
                                            ao[sl][:, oc * 512:(oc + 1) * 512],
                                            AF.Square,
                                            accum_out=CC[:, 4 + oc:5 + oc]))
                             for oc in range(OC)],
                            reads=[t_ao[sl]], writes=[t_CC])
                    em.task("dve",
                            [lambda v: v.reduce_sum(CC[:, 8:9], CC[:, 0:OC],
                                                    axis=AX.X),
                             lambda v: v.reduce_sum(CC[:, 9:10], CC[:, 4:4 + OC],
                                                    axis=AX.X),
                             lambda v: v.tensor_scalar_mul(CC[:, 10:11],
                                                           CC[:, 8:9], 1.0 / H),
                             lambda v: v.tensor_scalar_mul(CC[:, 11:12],
                                                           CC[:, 9:10], 1.0 / H),
                             em.flush_fn,
                             lambda v: v.tensor_mul(CC[:, 12:13], CC[:, 10:11],
                                                    CC[:, 10:11]),
                             em.flush_fn,
                             lambda v: v.tensor_sub(CC[:, 11:12], CC[:, 11:12],
                                                    CC[:, 12:13])],
                            reads=[t_CC], writes=[t_CC])
                    em.task("act",
                            [lambda sc: sc.activation(CC[:, 13:14], CC[:, 11:12],
                                                      AF.Ln, bias=cst_eps),
                             lambda sc: sc.copy(CC[:, 15:16], CC[:, 0:1]),
                             lambda sc: sc.activation(CC[:, 14:15], CC[:, 13:14],
                                                      AF.Exp, scale=-0.5)],
                            reads=[t_CC, t_eps], writes=[t_CC])
                    em.task("dve",
                            [lambda v, sl=sl: v.tensor_scalar(
                                 ytmp[sl], ao[sl], CC[:, 10:11], CC[:, 14:15],
                                 ALU.subtract, ALU.mult),
                             lambda v, sl=sl: v.tensor_add(ytmp[sl], ytmp[sl], xt[sl])],
                            reads=[t_ao[sl], t_CC, t_xt[sl]], writes=[t_ytmp[sl], t_CC])
                    em.dma("act", "yw", y_res[tc * 128:(tc + 1) * 128, :], ytmp[sl],
                           reads=[t_ytmp[sl]])
                    em.task("act",
                            [(lambda sc, sl=sl, i=i: sc.activation(
                                sqj2, ytmp[sl][:, i * 512:(i + 1) * 512], AF.Square,
                                accum_out=CC[:, 8 + i:9 + i]))
                             for i in range(H // 512)],
                            reads=[t_ytmp[sl]], writes=[t_CC])
                    em.task("dve",
                            [lambda v, sl=sl: v.reduce_sum(CC[:, 1:2], ytmp[sl],
                                                           axis=AX.X),
                             lambda v: v.reduce_sum(CC[:, 0:1],
                                                    CC[:, 8:8 + H // 512], axis=AX.X),
                             em.flush_fn,
                             lambda v: v.tensor_scalar_mul(CC[:, 2:3], CC[:, 1:2],
                                                           1.0 / H),
                             lambda v: v.tensor_scalar_mul(CC[:, 3:4], CC[:, 0:1],
                                                           1.0 / H),
                             em.flush_fn,
                             lambda v: v.tensor_mul(CC[:, 4:5], CC[:, 2:3],
                                                    CC[:, 2:3]),
                             em.flush_fn,
                             lambda v: v.tensor_sub(CC[:, 3:4], CC[:, 3:4],
                                                    CC[:, 4:5])],
                            reads=[t_ytmp[sl], t_CC], writes=[t_CC])
                    em.task("act",
                            [lambda sc: sc.activation(CC[:, 5:6], CC[:, 3:4],
                                                      AF.Ln, bias=cst_eps),
                             lambda sc: sc.copy(CC[:, 15:16], CC[:, 0:1]),
                             lambda sc: sc.activation(CC[:, 6:7], CC[:, 5:6],
                                                      AF.Exp, scale=-0.5)],
                            reads=[t_CC, t_eps], writes=[t_CC])
                    em.task("dve",
                            [lambda v, tc=tc: v.tensor_copy(
                                 m2r2[:, 2 * tc:2 * tc + 1], CC[:, 2:3]),
                             lambda v, tc=tc: v.tensor_copy(
                                 m2r2[:, 2 * tc + 1:2 * tc + 2], CC[:, 6:7])],
                            reads=[t_CC], writes=[t_m2r2, t_CC])

            em.deq_n(10 ** 9)
            em.barrier()

        with contextlib.ExitStack() as ph3:
            yn_fm = [sbuf(ph3, f"s_ynfm{dc}", [128, TK], BF16) for dc in range(DC)]
            t_ynfm = [Trk() for _ in range(DC)]
            x2 = [sbuf(ph3, f"s_x2_{i}", [128, TK], BF16) for i in range(HCB)]
            t_x2 = [Trk() for _ in range(HCB)]
            y2row = [sbuf(ph3, f"s_y2row{i}", [128, H], F32) for i in range(2)]
            t_y2row = [Trk(), Trk()]
            aupb = sbuf(ph3, "s_aupb", [128, DC * 16], BF16)
            t_aupb = Trk()
            adnb = sbuf(ph3, "s_adnb", [128, HC * 16], BF16)
            t_adnb = Trk()
            bdnb = sbuf(ph3, "s_bdnb", [16, H], BF16)
            t_bdnb = Trk()
            bup_sb = sbuf(ph3, "s_bup", [128, HC], F32)
            t_bup = Trk()
            t1b = sbuf(ph3, "s_t1b", [16, TK], BF16)
            t_t1b = Trk()
            t2a = sbuf(ph3, "s_t2a", [16, TK], F32)
            t_t2a = Trk()
            t2b = sbuf(ph3, "s_t2b", [16, TK], BF16)
            t_t2b = Trk()
            bdn_row = sbuf(ph3, "s_bdnrow", [1, H], BF16)
            t_bdnrow = Trk()
            _y3 = sbuf(ph3, "s_yrt3", [128, H], F32)
            _ty3 = Trk()
            yrt3 = [_y3, _y3]
            t_yrt3 = [_ty3, _ty3]
            cols3 = sbuf(ph3, "s_cols3", [128, 16], F32)
            t_cols3 = Trk()
            sqj3 = sbuf(ph3, "s_sqj3", [128, 512], BF16)
            t_y2hbm = [Trk() for _ in range(NTC)]

            em.dma("pool", "lor", bdn_row, dri["b_dn_bc"][0:1, :], writes=[t_bdnrow])
            em.dma("pool", "lor", aupb, dri["A_up_t"], writes=[t_aupb])
            em.dma("pool", "lor", adnb, dri["A_dn_t"], writes=[t_adnb])
            em.dma("pool", "lor", bdnb, dri["B_dn"], writes=[t_bdnb])
            em.dma("sp", "cst", bup_sb, dri["b_up_t"], writes=[t_bup])

            with contextlib.ExitStack() as ph3a:
                yt = [sbuf(ph3a, f"s_yt{i}", [128, H], F32) for i in range(2)]
                t_yt = [Trk(), Trk()]
                yn_tm = sbuf(ph3a, "s_yntm", [128, H], BF16)
                t_yntm = Trk()
                for tc in range(NTC):
                    sl = tc % 2
                    em.dma("sp", f"yr{sl}", yt[sl], y_res[tc * 128:(tc + 1) * 128, :],
                           writes=[t_yt[sl]])
                    em.task("dve",
                            [lambda v, sl=sl, tc=tc: v.tensor_scalar(
                                yn_tm, yt[sl], m2r2[:, 2 * tc:2 * tc + 1],
                                m2r2[:, 2 * tc + 1:2 * tc + 2],
                                ALU.subtract, ALU.mult)],
                            reads=[t_yt[sl], t_m2r2], writes=[t_yntm])
                    for g0 in range(0, DC, 8):
                        gdc = list(range(g0, min(g0 + 8, DC)))
                        pt = (pba, t_pba) if (g0 // 8) % 2 == 0 else (pbb, t_pbb)
                        em.task("te",
                                [(lambda t, dc=dc, j=j, pt=pt:
                                  t.transpose(pt[0][:, j * 128:(j + 1) * 128],
                                              yn_tm[:, dc * 128:(dc + 1) * 128], identb))
                                 for j, dc in enumerate(gdc)],
                                reads=[t_yntm, t_identb], writes=[pt[1]])
                        em.task("act",
                                [(lambda s, dc=dc, j=j, tc=tc, pt=pt:
                                  s.copy(yn_fm[dc][:, tc * 128:(tc + 1) * 128],
                                         pt[0][:, j * 128:(j + 1) * 128]))
                                 for j, dc in enumerate(gdc)],
                                reads=[pt[1]], writes=[t_ynfm[dc] for dc in gdc])

            em.barrier()
            wup_s = [[sbuf(ph3, f"s_wup{b}_{dc}", [128, HB // 4], BF16)
                      for dc in range(DC)] for b in range(3)]
            t_wup = [[Trk() for _ in range(DC)] for b in range(3)]
            wdn_s = [sbuf(ph3, f"s_wdn{i}", [128, H], BF16) for i in range(HCB)]
            t_wdn = [Trk() for _ in range(HCB)]
            bupb = sbuf(ph3, "s_bupb", [16, HB], BF16)
            t_bupb = Trk()

            if dbg:
                for dc in range(DC):
                    em.dma("pool", "natw", dyn[dc * 128:(dc + 1) * 128, :],
                           yn_fm[dc], reads=[t_ynfm[dc]])

            for tq in range(NTQ):
                fns = []
                for dc in range(DC):
                    fns.append(lambda t, dc=dc:
                               t.ldweights(aupb[:, dc * 16:(dc + 1) * 16]))
                    fns.append(lambda t, dc=dc, tq=tq:
                               t.matmul(pb[PSC][0:16, 0:512],
                                        aupb[:, dc * 16:(dc + 1) * 16],
                                        yn_fm[dc][:, tq * 512:(tq + 1) * 512],
                                        start=(dc == 0), stop=(dc == DC - 1)))
                em.task("te", fns, reads=[t_aupb] + [t_ynfm[d] for d in range(DC)],
                        writes=[t_pb[PSC]])
                em.task("act",
                        [lambda sc, tq=tq: sc.copy(t1b[:, tq * 512:(tq + 1) * 512],
                                                   pb[PSC][0:16, 0:512])],
                        reads=[t_pb[PSC]], writes=[t_t1b])

            QH = max(1, HCB // 4)

            def wup_load(hb, q):
                b = (hb * 4 + q) % 3
                for dc in range(DC):
                    em.dma("sp", "wup", wup_s[b][dc],
                           wup_nat[hb * HB + q * (HB // 4):
                                   hb * HB + (q + 1) * (HB // 4),
                                   dc * 128:(dc + 1) * 128],
                           writes=[t_wup[b][dc]], transpose=True)

            wup_load(0, 0)
            wup_load(0, 1)
            for hb in range(NHB):
                for hc in range(HCB):
                    em.dma("sp", "wdn", wdn_s[hc],
                           wdn_nat[:, (hb * HCB + hc) * 128:(hb * HCB + hc + 1) * 128],
                           writes=[t_wdn[hc]], transpose=True)
                em.dma("pool", "lor", bupb, dri["B_up"][:, hb * HB:(hb + 1) * HB],
                       writes=[t_bupb])
                nbk = min(NTQ, 4)
                HH = max(1, HCB // 2)
                for hc in range(HCB):
                    if hc % QH == 0:
                        sg = hb * 4 + hc // QH + 2
                        if sg < NHB * 4:
                            wup_load(sg // 4, sg % 4)
                    hcl = hc % QH
                    wb = (hb * 4 + hc // QH) % 3
                    hcg = hb * HCB + hc
                    for wv in range(0, NTQ, nbk):
                        tqs = list(range(wv, min(wv + nbk, NTQ)))
                        tq_pairs = ([tqs[:2], tqs[2:]] if len(tqs) >= 4
                                    else [tqs])
                        for pi, ptqs in enumerate(tq_pairs):
                            j0 = 2 * pi
                            fns = []
                            for dc in range(DC):
                                fns.append(lambda t, dc=dc, hcl=hcl, wb=wb:
                                           t.ldweights(
                                               wup_s[wb][dc][:, hcl * 128:(hcl + 1) * 128]))
                                for j, tq in enumerate(ptqs):
                                    fns.append(lambda t, dc=dc, hcl=hcl, j=j0 + j,
                                               tq=tq, wb=wb:
                                               t.matmul(pb[j],
                                                        wup_s[wb][dc][:, hcl * 128:
                                                                      (hcl + 1) * 128],
                                                        yn_fm[dc][:, tq * 512:(tq + 1) * 512],
                                                        start=(dc == 0), stop=False))
                            fns.append(lambda t, hc=hc:
                                       t.ldweights(bupb[:, hc * 128:(hc + 1) * 128]))
                            for j, tq in enumerate(ptqs):
                                fns.append(lambda t, hc=hc, j=j0 + j, tq=tq:
                                           t.matmul(pb[j], bupb[:, hc * 128:(hc + 1) * 128],
                                                    t1b[:, tq * 512:(tq + 1) * 512],
                                                    start=False, stop=True))
                            em.task("te", fns,
                                    reads=[t_wup[wb][dc] for dc in range(DC)] +
                                    [t_ynfm[d] for d in range(DC)] + [t_bupb, t_t1b],
                                    writes=[t_pb[j0 + j] for j in range(len(ptqs))])
                            em.task("act",
                                    [(lambda sc, hc=hc, hcg=hcg, j=j0 + j, tq=tq:
                                      sc.activation(x2[hc][:, tq * 512:(tq + 1) * 512],
                                                    pb[j], AF.Relu,
                                                    bias=bup_sb[:, hcg:hcg + 1]))
                                     for j, tq in enumerate(ptqs)],
                                    reads=[t_pb[j0 + j] for j in range(len(ptqs))] + [t_bup],
                                    writes=[t_x2[hc]])
                if dbg and hb == 0:
                    for hc in range(HCB):
                        em.dma("pool", "natw", dx2[hc * 128:(hc + 1) * 128, :],
                               x2[hc], reads=[t_x2[hc]])
                    em.dma("pool", "natw", dt1, t1b, reads=[t_t1b])
                def emit_t2(tq):
                    fns = []
                    for hc in range(HCB):
                        hcg = hb * HCB + hc
                        fns.append(lambda t, hcg=hcg:
                                   t.ldweights(adnb[:, hcg * 16:(hcg + 1) * 16]))
                        fns.append(lambda t, hcg=hcg, hc=hc, tq=tq:
                                   t.matmul(pb[PSC][0:16, 0:512],
                                            adnb[:, hcg * 16:(hcg + 1) * 16],
                                            x2[hc][:, tq * 512:(tq + 1) * 512],
                                            start=(hc == 0), stop=(hc == HCB - 1)))
                    em.task("te", fns,
                            reads=[t_adnb] + [t_x2[hc] for hc in range(HCB)],
                            writes=[t_pb[PSC]])
                    if hb == 0:
                        em.task("dve",
                                [lambda v, tq=tq: v.tensor_copy(
                                    t2a[:, tq * 512:(tq + 1) * 512],
                                    pb[PSC][0:16, 0:512])],
                                reads=[t_pb[PSC]], writes=[t_t2a])
                    else:
                        em.task("dve",
                                [lambda v, tq=tq: v.tensor_add(
                                    t2a[:, tq * 512:(tq + 1) * 512],
                                    t2a[:, tq * 512:(tq + 1) * 512],
                                    pb[PSC][0:16, 0:512])],
                                reads=[t_pb[PSC]], writes=[t_t2a])
                if hb == NHB - 1:
                    for tq in range(NTQ):
                        emit_t2(tq)
                    em.task("dve", [lambda v: v.tensor_copy(t2b, t2a)],
                            reads=[t_t2a], writes=[t_t2b])
                if hb > 0:
                    em.wait_all_dma("pool", "y2a")
                if hb == NHB - 1:
                    em.wait_all_dma("sp", "y2a")
                for tc in range(NTC):
                    if hb < NHB - 1 and tc < NTQ:
                        emit_t2(tc)
                    last_hb = (hb == NHB - 1)
                    for ocs in ([[0, 1], [2, 3]] if OC >= 4 else [list(range(OC))]):
                        fns = []
                        for hc in range(HCB):
                            fns.append(lambda t, hc=hc, tc=tc:
                                       t.ldweights(x2[hc][:, tc * 128:(tc + 1) * 128]))
                            for oc in ocs:
                                stop = (hc == HCB - 1) and not last_hb
                                fns.append(lambda t, hc=hc, oc=oc, tc=tc, stop=stop:
                                           t.matmul(pb[oc],
                                                    x2[hc][:, tc * 128:(tc + 1) * 128],
                                                    wdn_s[hc][:, oc * 512:(oc + 1) * 512],
                                                    start=(hc == 0), stop=stop))
                        if last_hb:
                            fns.append(lambda t, tc=tc:
                                       t.ldweights(t2b[:, tc * 128:(tc + 1) * 128]))
                            for oc in ocs:
                                fns.append(lambda t, tc=tc, oc=oc:
                                           t.matmul(pb[oc],
                                                    t2b[:, tc * 128:(tc + 1) * 128],
                                                    bdnb[:, oc * 512:(oc + 1) * 512],
                                                    start=False, stop=False))
                            fns.append(lambda t: t.ldweights(ones_row))
                            for oc in ocs:
                                fns.append(lambda t, oc=oc:
                                           t.matmul(pb[oc], ones_row,
                                                    bdn_row[0:1, oc * 512:(oc + 1) * 512],
                                                    start=False, stop=True))
                        em.task("te", fns,
                                reads=[t_x2[hc] for hc in range(HCB)] +
                                [t_wdn[hc] for hc in range(HCB)] +
                                ([t_t2b, t_bdnb, t_bdnrow, t_oner] if last_hb else []),
                                writes=[t_pb[oc] for oc in ocs])
                    yp = tc % 2
                    if not last_hb:
                        for ocs in ([[0, 1], [2, 3]] if OC >= 4
                                    else [list(range(OC))]):
                            em.task("act",
                                    [(lambda sc, oc=oc, yp=yp: sc.copy(
                                        y2row[yp][:, oc * 512:(oc + 1) * 512], pb[oc]))
                                     for oc in ocs],
                                    reads=[t_pb[oc] for oc in ocs],
                                    writes=[t_y2row[yp]])
                        kw = {} if hb == 0 else {"accum_op": ALU.add}
                        em.dma("pool", "y2a", y2_hbm[tc * 128:(tc + 1) * 128, :],
                               y2row[yp], reads=[t_y2row[yp]],
                               writes=[t_y2hbm[tc]], **kw)
                    else:
                        em.dma("sp", f"y2r{yp}", y2row[yp],
                               y2_hbm[tc * 128:(tc + 1) * 128, :],
                               writes=[t_y2row[yp]])
                        em.dma("act", f"yrf{yp}", yrt3[yp],
                               y_res[tc * 128:(tc + 1) * 128, :], writes=[t_yrt3[yp]])
                        for ocs in ([[0, 1], [2, 3]] if OC >= 4
                                    else [list(range(OC))]):
                            em.task("dve",
                                    [(lambda v, oc=oc, yp=yp: v.scalar_tensor_tensor(
                                        y2row[yp][:, oc * 512:(oc + 1) * 512],
                                        pb[oc], 1.0,
                                        y2row[yp][:, oc * 512:(oc + 1) * 512],
                                        ALU.mult, ALU.add))
                                     for oc in ocs],
                                    reads=[t_pb[oc] for oc in ocs] +
                                    [t_y2row[yp]],
                                    writes=[t_y2row[yp]])
                        em.task("act",
                                [(lambda sc, i=i, yp=yp: sc.activation(
                                    sqj3, y2row[yp][:, i * 512:(i + 1) * 512], AF.Square,
                                    accum_out=cols3[:, 8 + i:9 + i]))
                                 for i in range(H // 512)],
                                reads=[t_y2row[yp]], writes=[t_cols3])
                        em.task("dve",
                                [lambda v, yp=yp: v.reduce_sum(cols3[:, 1:2], y2row[yp], axis=AX.X),
                                 lambda v: v.reduce_sum(cols3[:, 0:1],
                                                        cols3[:, 8:8 + H // 512],
                                                        axis=AX.X),
                                 em.flush_fn,
                                 lambda v: v.tensor_scalar_mul(cols3[:, 2:3],
                                                               cols3[:, 1:2], 1.0 / H),
                                 lambda v: v.tensor_scalar_mul(cols3[:, 3:4],
                                                               cols3[:, 0:1], 1.0 / H),
                                 em.flush_fn,
                                 lambda v: v.tensor_mul(cols3[:, 4:5], cols3[:, 2:3],
                                                        cols3[:, 2:3]),
                                 em.flush_fn,
                                 lambda v: v.tensor_sub(cols3[:, 3:4], cols3[:, 3:4],
                                                        cols3[:, 4:5])],
                                reads=[t_y2row[yp], t_cols3], writes=[t_cols3])
                        em.task("act",
                                [lambda sc: sc.activation(cols3[:, 5:6], cols3[:, 3:4],
                                                          AF.Ln, bias=cst_eps),
                                 lambda sc: sc.copy(cols3[:, 15:16], cols3[:, 0:1]),
                                 lambda sc: sc.activation(cols3[:, 6:7], cols3[:, 5:6],
                                                          AF.Exp, scale=-0.5)],
                                reads=[t_cols3, t_eps], writes=[t_cols3])
                        em.task("dve",
                                [lambda v, yp=yp: v.tensor_scalar(
                                    y2row[yp], y2row[yp], cols3[:, 2:3], cols3[:, 6:7],
                                    ALU.subtract, ALU.mult),
                                 lambda v, yp=yp: v.tensor_add(y2row[yp], y2row[yp],
                                                               yrt3[yp])],
                                reads=[t_cols3, t_yrt3[yp]], writes=[t_y2row[yp]])
                        em.dma("act", "outw", out_d[tc * 128:(tc + 1) * 128, :], y2row[yp],
                               reads=[t_y2row[yp]])

        em.barrier()

        em.wait_all_dma("sp", "outw")

        with nc.Block() as block:
            em.replay(block)
    return nc


def _per_core_inputs(inputs, core, n_half=2):
    b, half = core // 2, core % 2
    x = inputs["L0_attn_x"]
    S, H = x.shape[1], x.shape[2]
    TK = S // n_half
    warm = np.asarray(x[b, TK - WARM:TK, :], np.float32)
    kept = np.asarray(x[b, half * TK:(half + 1) * TK, :], np.float32)
    DC = H // 128
    HF = inputs["L0_ffn_w_up_q"].shape[0]
    HC = HF // 128
    return {
        "x_att": np.ascontiguousarray(np.concatenate([warm, kept], 0)),
        "wscale": np.full((128, 1), 1.0 if half == 1 else 0.0, np.float32),
        "identb": np.eye(128, dtype=BF),
        "masku8": np.tile((np.arange(128)[:, None] <= np.arange(128)[None, :])
                          .astype(np.uint8), (1, SEC // 128)),
        "Wq": np.asarray(inputs["L0_attn_Wq"], np.float32),
        "Wf": np.asarray(inputs["L0_attn_Wf"], np.float32),
        "Wi": np.asarray(inputs["L0_attn_Wi"], np.float32),
        "Wo": np.asarray(inputs["L0_attn_Wo"], np.float32),
        "gw_t": np.ascontiguousarray(
            np.asarray(inputs["L0_attn_g_weight"], np.float32).reshape(DC, 128).T),
        "b_up_t": np.ascontiguousarray(
            np.asarray(inputs["L0_ffn_b_up"], np.float32).reshape(HC, 128).T),
        "b_dn_bc": np.ascontiguousarray(np.broadcast_to(
            np.asarray(inputs["L0_ffn_b_down"], np.float32)[None, :], (128, H))),
        "A_up_t": np.ascontiguousarray(
            np.asarray(inputs["L0_ffn_w_up_lora_a"], np.float32)
            .reshape(DC, 128, 16).transpose(1, 0, 2).reshape(128, DC * 16)),
        "B_up": np.asarray(inputs["L0_ffn_w_up_lora_b"], np.float32),
        "A_dn_t": np.ascontiguousarray(
            np.asarray(inputs["L0_ffn_w_down_lora_a"], np.float32)
            .reshape(HC, 128, 16).transpose(1, 0, 2).reshape(128, HC * 16)),
        "B_dn": np.asarray(inputs["L0_ffn_w_down_lora_b"], np.float32),
        "wuq": np.asarray(inputs["L0_ffn_w_up_q"], np.int32),
        "wus": np.asarray(inputs["L0_ffn_w_up_scale"], np.float32),
        "wdq": np.asarray(inputs["L0_ffn_w_down_q"], np.int32),
        "wds": np.asarray(inputs["L0_ffn_w_down_scale"], np.float32),
    }


def kernel(**inputs):
    x = inputs["L0_attn_x"]
    B, S, H = x.shape
    TK = S // 2
    nc = build_nc(H=H, HF=inputs["L0_ffn_w_up_q"].shape[0], TK=TK)
    in_maps = [_per_core_inputs(inputs, c) for c in range(8)]
    res = run_bass_kernel_spmd(nc, in_maps, core_ids=list(range(8)))
    out = np.empty((B, S, H), np.float32)
    for c in range(8):
        b, half = c // 2, c % 2
        out[b, half * TK:(half + 1) * TK, :] = np.asarray(res.results[c]["out"])
    return out
